# version 18
# baseline (speedup 1.0000x reference)
import numpy as np
import ml_dtypes
from contextlib import ExitStack

import concourse.bass as bass
import concourse.mybir as mybir
from concourse.bass_utils import run_bass_kernel_spmd

F32 = mybir.dt.float32
BF16 = mybir.dt.bfloat16
AF = mybir.ActivationFunctionType
ALU = mybir.AluOpType
AX = mybir.AxisListType

S, D = 4096, 2048
H, HD, LAT, PD, DK = 16, 128, 512, 64, 192
II = 8192
CS = float(1.0 / np.sqrt(DK))
EPS = 1e-5
NT, NTO = 32, 16
NG, NGO = 8, 4
NSC = 16
NBLK = 2

SEM_NAMES = [
    "c_l", "c_w", "s_x", "s_lns", "s_var", "s_rstd", "s_ln",
    "s_tp", "s_tpe", "s_kr", "s_krr", "s_krt", "s_krte",
    "s_b", "s_be", "s_c", "s_ce", "s_dmm", "s_de",
    "s_b2", "s_b2e", "s_c2", "s_c2e",
    "s_qr", "s_qrr", "s_qrt", "s_qrte",
    "s_kw", "s_vw", "s_qw", "s_qrw", "s_xnw",
    "bar",
    "s_kl", "s_vl0", "s_vl1", "s_vl2", "s_vl3", "s_ql", "s_qrl",
    "s_smm", "s_exp", "s_pv", "s_oev", "s_ost", "s_hdone",
    "s_ol", "s_ol2", "s_wo", "s_wo2", "s_s4mm", "s_xl", "s_s4add", "s_resw", "c_w2",
    "s_acc", "s_xnl", "s_wg", "s_wu", "s_wdn", "s_gu", "s_silu", "s_hmul", "s_mask",
    "s_down", "s_add", "s_scd", "s_out", "s_dvef", "s_rs", "s_pad",
]

ENGS = ["tensor", "scalar", "vector", "gpsimd", "sync"]


def _mkW(eng, sems):
    known = {}

    def W(name, val):
        if val <= 0:
            return
        if known.get(name, 0) >= val:
            return
        eng.wait_ge(sems[name], val)
        known[name] = val
    return W


def _mkFL(e, sems):
    cnt = [0]

    def FL(ins):
        cnt[0] += 1
        ins.then_inc(sems["s_dvef"], 1)
        e.wait_ge(sems["s_dvef"], cnt[0])
    return FL


def _mkDMA(e, W, sems):
    cnt = {}

    def DMA(semname, out, in_):
        c = cnt.get(semname, 0)
        if c > 0:
            W(semname, c)
        e.dma_start(out=out, in_=in_).then_inc(sems[semname], 16)
        cnt[semname] = c + 16
    return DMA


def build_nc(debug=False, nstages=4, ng=8, detect_races=True):
    NG = ng
    NGO = min(ng, 4)
    NT = 4 * NG
    NTO = 4 * NGO
    nc = bass.Bass("TRN2", target_bir_lowering=False,
                   detect_race_conditions=detect_races)

    def din(name, shape, dt=F32):
        return nc.dram_tensor(name, shape, dt, kind="ExternalInput").ap()

    xb = din("xb", [S, D])
    wdkv = din("wdkv", [D, LAT]); wdq = din("wdq", [D, LAT])
    wuk = din("wuk", [LAT, D]); wuv = din("wuv", [LAT, D]); wuq = din("wuq", [LAT, D])
    wkr = din("wkr", [D, PD]); wqr = din("wqr", [LAT, H * PD])
    wo = din("wo", [D, D])
    wg = din("wg", [D, II]); wu = din("wu", [D, II]); wd = din("wd", [II, D])
    cosT = din("cosT", [S, PD], BF16); ssinT = din("ssinT", [S, PD], BF16)
    identD = din("identD", [128, 128], BF16)
    onesD = din("onesD", [128, 128], BF16)
    out = nc.dram_tensor("out", [2048, D], F32, kind="ExternalOutput").ap()

    skind = "ExternalOutput" if debug else "Internal"
    krdbg = nc.dram_tensor("krdbg", [64, S], BF16, kind=skind).ap() if debug else None
    s2dbg = nc.dram_tensor("s2dbg", [128, 8 * H], F32, kind=skind).ap() if debug else None
    kscr = nc.dram_tensor("kscr", [H, 128, S], BF16, kind=skind).ap()
    vscr = nc.dram_tensor("vscr", [32, 128, D], BF16, kind=skind).ap()
    qscr = nc.dram_tensor("qscr", [H, 128, 2048], BF16, kind=skind).ap()
    qrscr = nc.dram_tensor("qrscr", [8, 128, 2048], BF16, kind=skind).ap()
    xnscr = nc.dram_tensor("xnscr", [4, 128, 8192], BF16, kind=skind).ap()
    oscr = nc.dram_tensor("oscr", [H, 128, 2048], BF16, kind=skind).ap()
    resscr = nc.dram_tensor("resscr", [16, 128, D], F32, kind=skind).ap()

    es = ExitStack()
    sems = {}
    with es:
        for n in SEM_NAMES:
            sems[n] = es.enter_context(nc.semaphore(n))

        def SB(name, shape, dt=F32):
            return es.enter_context(nc.sbuf_tensor(name, shape, dt))

        def PS(name, shape):
            return es.enter_context(nc.psum_tensor(name, shape, F32))


        barv = [0]

        def bar_end(eng, W):
            eng.nop().then_inc(sems["bar"], 1)

        def bar_begin(W, k):
            W("bar", 5 * k)

        with ExitStack() as es12:
            def SB12(name, shape, dt=F32):
                return es12.enter_context(nc.sbuf_tensor(name, shape, dt))

            kr_res = SB12("kr_res", [64, S], BF16)
            ones = SB12("ones", [128, 128], BF16)
            ident = SB12("ident", [128, 128], BF16)
            eps_sb = SB12("eps_sb", [128, 1])

            with ExitStack() as e1:
                def SB1(name, shape, dt=F32):
                    return e1.enter_context(nc.sbuf_tensor(name, shape, dt))

                def PS1(name, shape):
                    return e1.enter_context(nc.psum_tensor(name, shape, F32))

                wdkv_sb = SB1("wdkv_sb", [128, 16 * 512], BF16)
                wdq_sb = SB1("wdq_sb", [128, 16 * 512], BF16)
                wuk_sb = SB1("wuk_sb", [128, 4 * 2048], BF16)
                wuv_sb = SB1("wuv_sb", [128, 4 * 2048], BF16)
                wuq_sb = SB1("wuq_sb", [128, 4 * 2048], BF16)
                wkr_sb = SB1("wkr_sb", [128, 16 * 64], BF16)
                wqr_sb = SB1("wqr_sb", [128, 4 * 1024], BF16)
                cos_sb = SB1("cos_sb", [128, NT * 64], BF16)
                ssin_sb = SB1("ssin_sb", [128, NT * 64], BF16)
                xsb = [SB1(f"xsb{i}", [128, D]) for i in range(3)]
                scrap_bf = SB1("scrap_bf", [128, D], mybir.dt.float8e4)
                scrap2_bf = scrap_bf
                stats = SB1("stats", [128, 6])
                mean_sb = SB1("mean_sb", [128, 3])
                rstd_sb = SB1("rstd_sb", [128, 3])
                var_sb = SB1("var_sb", [128, 3])
                rln_sb = SB1("rln_sb", [128, 3])
                tmp_sb = SB1("tmp_sb", [128, 3])
                xn_bf = [SB1(f"xn_bf{i}", [128, D], BF16) for i in range(3)]
                xnsb = SB1("xnsb", [128, 16 * 512], BF16)
                ckv_sb = SB1("ckv_sb", [128, 4 * 512], BF16)
                cq_sb = SB1("cq_sb", [128, 4 * 512], BF16)
                kq_stage = SB1("kq_stage", [128, 16 * 512], BF16)
                vsb = [SB1(f"vsb{i}", [128, D], BF16) for i in range(2)]
                qrsb = SB1("qrsb", [128, 8 * 512], BF16)
                kr_tm = SB1("kr_tm", [128, 4 * 64], BF16)
                qr_bf = SB1("qr_bf", [128, 1024], BF16)
                rop1 = SB1("rop1", [128, 512])
                rop2 = SB1("rop2", [128, 512])

                TB = [PS1("TB0", [128, 512]), PS1("TB1", [128, 512])]
                CB = [PS1("CB0", [128, 512]), PS1("CB1", [128, 512])]
                KB_ = [PS1("KB0", [128, 512]), PS1("KB1", [128, 512])]
                VB = [PS1("VB0", [128, 512]), PS1("VB1", [128, 512])]

                v_wdkv = wdkv_sb[:].rearrange("p (dc l) -> p dc l", dc=16)
                v_wdq = wdq_sb[:].rearrange("p (dc l) -> p dc l", dc=16)
                v_wuk = wuk_sb[:].rearrange("p (lc c) -> p lc c", lc=4)
                v_wuv = wuv_sb[:].rearrange("p (lc c) -> p lc c", lc=4)
                v_wuq = wuq_sb[:].rearrange("p (lc c) -> p lc c", lc=4)
                v_wkr = wkr_sb[:].rearrange("p (dc c) -> p dc c", dc=16)
                v_wqr = wqr_sb[:].rearrange("p (lc c) -> p lc c", lc=4)
                v_xnsb = xnsb[:].rearrange("p (dc j) -> p dc j", dc=16)
                v_ckv = ckv_sb[:].rearrange("p (lc j) -> p lc j", lc=4)
                v_cq = cq_sb[:].rearrange("p (lc j) -> p lc j", lc=4)
                v_kq = kq_stage[:].rearrange("p (cc j) -> p cc j", cc=16)
                v_cos = cos_sb[:].rearrange("p (t d) -> p t d", t=NT)
                v_ssin = ssin_sb[:].rearrange("p (t d) -> p t d", t=NT)
                v_krtm = kr_tm[:].rearrange("p (t d) -> p t d", t=4)
                v_qrsb = qrsb[:].rearrange("p (c j) -> p c j", c=8)

                def own(g):
                    return g < NGO

                CW_TOT = 16 * (2 + 2 + 1 + 1 + 1 + 2 + 1)

                with nc.Block() as b1:
                    @b1.sync
                    def _(e):
                        W = _mkW(e, sems)
                        DMA = _mkDMA(e, W, sems)
                        DMA("c_l", ident[:], identD)
                        DMA("c_l", ones[:], onesD)
                        for t in range(3):
                            DMA("s_x", xsb[t % 3][:], xb[t * 128:(t + 1) * 128, :])
                        DMA("c_l", cos_sb[:].rearrange("p (t d) -> p t d", t=NT), cosT[0:NT * 128, :].rearrange("(t p) d -> p t d", p=128))
                        DMA("c_l", ssin_sb[:].rearrange("p (t d) -> p t d", t=NT), ssinT[0:NT * 128, :].rearrange("(t p) d -> p t d", p=128))
                        for t in range(3, NT):
                            W("s_ln", t - 2)
                            DMA("s_x", xsb[t % 3][:], xb[t * 128:(t + 1) * 128, :])
                        bar_end(e, W)

                    @b1.gpsimd
                    def _(e):
                        W = _mkW(e, sems)
                        DMA = _mkDMA(e, W, sems)
                        loads = [
                            (wdkv_sb, wdkv.rearrange("(dc p) l -> p dc l", p=128), 16, 2),
                            (wdq_sb, wdq.rearrange("(dc p) l -> p dc l", p=128), 16, 2),
                            (wuk_sb, wuk.rearrange("(lc p) c -> p lc c", p=128), 4, 1),
                            (wuv_sb, wuv.rearrange("(lc p) c -> p lc c", p=128), 4, 1),
                            (wuq_sb, wuq.rearrange("(lc p) c -> p lc c", p=128), 4, 1),
                            (wkr_sb, wkr.rearrange("(dc p) c -> p dc c", p=128), 16, 2),
                            (wqr_sb, wqr.rearrange("(lc p) c -> p lc c", p=128), 4, 1),
                        ]
                        ncw = [0]
                        for sb_t, src, nch, nsplit in loads:
                            dst = sb_t[:].rearrange("p (a c) -> p a c", a=nch)
                            step = nch // nsplit
                            for sp in range(nsplit):
                                DMA("c_w" if ncw[0] % 2 == 0 else "c_w2",
                                    dst[:, sp * step:(sp + 1) * step, :],
                                    src[:, sp * step:(sp + 1) * step, :])
                                ncw[0] += 1
                        for g in range(NG):
                            W("s_ce", 16 * (g + 1))
                            for half in range(2):
                                DMA("s_kw", kscr[:, :, g * 512:(g + 1) * 512]
                                        .rearrange("h p j -> p h j")[:, half * 8:(half + 1) * 8, :], v_kq[:, half * 8:(half + 1) * 8, :])
                            for t4 in range(4):
                                t = 4 * g + t4
                                W("s_de", 16 * g + 4 * (t4 + 1))
                                DMA("s_vw", vscr[t], vsb[t4 % 2][:])
                            if own(g):
                                W("s_c2e", 16 * (g + 1))
                                for half in range(2):
                                    DMA("s_qw", qscr[:, :, g * 512:(g + 1) * 512]
                                            .rearrange("h p j -> p h j")[:, half * 8:(half + 1) * 8, :], v_kq[:, half * 8:(half + 1) * 8, :])
                                W("s_qrte", 4 * (g + 1))
                                DMA("s_qrw", qrscr[:, :, g * 512:(g + 1) * 512].rearrange("c p j -> p c j"), v_qrsb)
                                W("s_tpe", 16 * (g + 1))
                                DMA("s_xnw", xnscr[g], xnsb[:])
                        if debug:
                            W("s_krte", NG)
                            DMA("c_w", krdbg[:, 0:NG * 512], kr_res[:, 0:NG * 512])
                            W("c_w", 96)
                        W("s_kw", 16 * NG * 2)
                        W("s_vw", 16 * NT)
                        W("s_qw", 16 * NGO * 2)
                        W("s_qrw", 16 * NGO)
                        W("s_xnw", 16 * NGO)
                        bar_end(e, W)

                    @b1.scalar
                    def _(e):
                        W = _mkW(e, sems)
                        DMA = _mkDMA(e, W, sems)

                        def stats_op(t):
                            if t >= NT:
                                return
                            W("s_x", 16 * (t + 1))
                            W("s_var", t - 2)
                            e.activation(scrap_bf[:], xsb[t % 3][:], AF.Square,
                                         accum_out=stats[:, 2 * (t % 3):2 * (t % 3) + 1])
                            e.activation(scrap2_bf[:], xsb[t % 3][:], AF.Identity,
                                         accum_out=stats[:, 2 * (t % 3) + 1:2 * (t % 3) + 2]
                                         ).then_inc(sems["s_lns"], 1)

                        def rstd_op(t):
                            if t >= NT:
                                return
                            W("s_var", t + 1)
                            W("s_ln", t - 2)
                            e.activation(rln_sb[:, t % 3:t % 3 + 1], var_sb[:, t % 3:t % 3 + 1],
                                         AF.Ln, bias=eps_sb[:, 0:1])
                            e.activation(rstd_sb[:, t % 3:t % 3 + 1], rln_sb[:, t % 3:t % 3 + 1],
                                         AF.Exp, scale=-0.5).then_inc(sems["s_rstd"], 1)

                        stats_op(0)
                        for g in range(NG):
                            for t4 in range(4):
                                t = 4 * g + t4
                                stats_op(t + 1)
                                rstd_op(t)
                                for tg in range(4):
                                    idx = 4 * t + tg
                                    W("s_tp", idx + 1)
                                    if t4 == 0 and tg == 0 and g >= 1 and own(g - 1):
                                        W("s_xnw", 16 * g)
                                    e.copy(v_xnsb[:, tg * 4:(tg + 1) * 4, t4 * 128:(t4 + 1) * 128],
                                           TB[idx % 2][:].rearrange("p (c j) -> p c j", c=4)
                                           ).then_inc(sems["s_tpe"], 1)
                            if own(g):
                                for t4 in range(4):
                                    W("s_qrt", 4 * g + t4 + 1)
                                    if t4 == 0 and g >= 1:
                                        W("s_qrw", 16 * g)
                                    e.copy(v_qrsb[:, 0:8:2, t4 * 128:(t4 + 1) * 128],
                                           TB[0][:].rearrange("p (c j) -> p c j", c=4))
                                    e.copy(v_qrsb[:, 1:8:2, t4 * 128:(t4 + 1) * 128],
                                           TB[1][:].rearrange("p (c j) -> p c j", c=4)
                                           ).then_inc(sems["s_qrte"], 1)
                        bar_end(e, W)

                    @b1.vector
                    def _(e):
                        W = _mkW(e, sems)
                        DMA = _mkDMA(e, W, sems)
                        FL = _mkFL(e, sems)

                        def var_op(t):
                            if t >= NT:
                                return
                            W("s_lns", t + 1)
                            W("s_ln", t - 2)
                            sl = slice(t % 3, t % 3 + 1)
                            ssq = stats[:, 2 * (t % 3):2 * (t % 3) + 1]
                            ssum = stats[:, 2 * (t % 3) + 1:2 * (t % 3) + 2]
                            FL(e.tensor_scalar_mul(mean_sb[:, sl], ssum, 1.0 / D))
                            e.tensor_scalar_mul(var_sb[:, sl], ssq, 1.0 / D)
                            FL(e.tensor_mul(tmp_sb[:, sl], mean_sb[:, sl], mean_sb[:, sl]))
                            e.tensor_sub(var_sb[:, sl], var_sb[:, sl], tmp_sb[:, sl]
                                         ).then_inc(sems["s_var"], 1)

                        def norm_op(t):
                            if t >= NT:
                                return
                            W("s_rstd", t + 1)
                            W("s_tp", 4 * (t - 2))
                            sl = slice(t % 3, t % 3 + 1)
                            e.tensor_scalar(xn_bf[t % 3][:], xsb[t % 3][:],
                                            mean_sb[:, sl], rstd_sb[:, sl],
                                            ALU.subtract, ALU.mult).then_inc(sems["s_ln"], 1)

                        def rope_k(t):
                            g, t4 = t // 4, t % 4
                            W("c_l", 64)
                            W("s_kr", t + 1)
                            if t4 == 0:
                                W("s_krt", g)
                            src = CB[t % 2][:, 0:64]
                            e.tensor_mul(rop1[:, 0:64], src, v_cos[:, t, :])
                            e.tensor_mul(rop2[:, 0:32], CB[t % 2][:, 32:64], v_ssin[:, t, 0:32])
                            e.tensor_mul(rop2[:, 32:64], CB[t % 2][:, 0:32], v_ssin[:, t, 32:64])
                            e.tensor_add(v_krtm[:, t4, :], rop1[:, 0:64], rop2[:, 0:64]
                                         ).then_inc(sems["s_krr"], 1)

                        e.memset(eps_sb[:], EPS)
                        var_op(0)
                        var_op(1)
                        norm_op(0)
                        for g in range(NG):
                            for t4 in range(4):
                                t = 4 * g + t4
                                rope_k(t)
                                if t4 < 3:
                                    var_op(t + 2)
                                    norm_op(t + 1)
                            for lc in range(4):
                                W("s_b", 4 * g + lc + 1)
                                e.tensor_copy(v_ckv[:, lc, :], CB[lc % 2][:]).then_inc(sems["s_be"], 1)
                            for cc in range(16):
                                W("s_c", 16 * g + cc + 1)
                                if cc == 0:
                                    W("s_kw", 32 * g)
                                    if g >= 1 and own(g - 1):
                                        W("s_qw", 32 * g)
                                e.tensor_copy(v_kq[:, cc, :], KB_[cc % 2][:]).then_inc(sems["s_ce"], 1)
                            for t4 in range(4):
                                for cg in range(4):
                                    i = 4 * t4 + cg
                                    W("s_dmm", 16 * g + i + 1)
                                    if cg == 0:
                                        W("s_vw", 16 * (4 * g + t4 - 1))
                                    e.tensor_copy(vsb[t4 % 2][:, cg * 512:(cg + 1) * 512],
                                                  VB[i % 2][:]).then_inc(sems["s_de"], 1)
                            if own(g):
                                for lc in range(4):
                                    W("s_b2", 4 * g + lc + 1)
                                    e.tensor_copy(v_cq[:, lc, :], CB[lc % 2][:]).then_inc(sems["s_b2e"], 1)
                                for cc in range(16):
                                    W("s_c2", 16 * g + cc + 1)
                                    if cc == 0:
                                        W("s_kw", 32 * (g + 1))
                                    e.tensor_copy(v_kq[:, cc, :], KB_[cc % 2][:]).then_inc(sems["s_c2e"], 1)
                            if own(g):
                                for t4 in range(4):
                                    t = 4 * g + t4
                                    W("s_qr", 8 * g + 2 * t4 + 2)
                                    W("s_qrt", 4 * g + t4)
                                    for cc in range(2):
                                        vps = VB[cc][:].rearrange("p (h d) -> p h d", h=8)
                                        cosb = v_cos[:, t, :].unsqueeze(1).broadcast_to([128, 8, 64])
                                        ssb0 = v_ssin[:, t, 0:32].unsqueeze(1).broadcast_to([128, 8, 32])
                                        ssb1 = v_ssin[:, t, 32:64].unsqueeze(1).broadcast_to([128, 8, 32])
                                        r1 = rop1[:, 0:512].rearrange("p (h d) -> p h d", h=8)
                                        r2 = rop2[:, 0:512].rearrange("p (h d) -> p h d", h=8)
                                        e.tensor_mul(r1, vps, cosb)
                                        e.tensor_mul(r2[:, :, 0:32], vps[:, :, 32:64], ssb0)
                                        e.tensor_mul(r2[:, :, 32:64], vps[:, :, 0:32], ssb1)
                                        e.tensor_add(qr_bf[:, cc * 512:(cc + 1) * 512],
                                                     rop1[:, 0:512], rop2[:, 0:512])
                                    e.engine_nop().then_inc(sems["s_qrr"], 1)
                            W("s_krt", g + 1)
                            e.tensor_copy(kr_res[:, g * 512:(g + 1) * 512],
                                          KB_[0][0:64, :]).then_inc(sems["s_krte"], 1)
                            var_op(4 * g + 5)
                            norm_op(4 * g + 4)
                        bar_end(e, W)

                    @b1.tensor
                    def _(e):
                        W = _mkW(e, sems)
                        DMA = _mkDMA(e, W, sems)
                        for g in range(NG):
                            for t4 in range(4):
                                t = 4 * g + t4
                                W("s_ln", t + 1)
                                if g >= 1 and own(g - 1):
                                    W("s_qrte", 4 * g)
                                for tg in range(4):
                                    idx = 4 * t + tg
                                    W("s_tpe", idx - 1)
                                    W("c_l", 16)
                                    for j in range(4):
                                        ins = e.matmul(
                                            TB[idx % 2][:, j * 128:(j + 1) * 128],
                                            xn_bf[t % 3][:, (tg * 4 + j) * 128:(tg * 4 + j + 1) * 128],
                                            ident[:], start=True, stop=True)
                                        if j == 3:
                                            ins.then_inc(sems["s_tp"], 1)
                                W("s_tpe", 4 * (t + 1))
                                W("c_w", 80)
                                W("c_w2", 80)
                                W("s_krr", t - 1)
                                for dc in range(16):
                                    ins = e.matmul(CB[t % 2][:, 0:64],
                                                   v_xnsb[:, dc, t4 * 128:(t4 + 1) * 128],
                                                   v_wkr[:, dc, :],
                                                   start=(dc == 0), stop=(dc == 15))
                                    if dc == 15:
                                        ins.then_inc(sems["s_kr"], 1)
                            W("s_tpe", 16 * (g + 1))
                            W("s_krr", 4 * g + 4)
                            for lc in range(4):
                                W("s_be", 4 * g + lc - 1)
                                for dc in range(16):
                                    ins = e.matmul(CB[lc % 2][:],
                                                   v_wdkv[:, dc, lc * 128:(lc + 1) * 128],
                                                   v_xnsb[:, dc, :],
                                                   start=(dc == 0), stop=(dc == 15))
                                    if dc == 15:
                                        ins.then_inc(sems["s_b"], 1)
                            W("s_be", 4 * (g + 1))
                            W("s_krte", g)
                            for cc in range(16):
                                W("s_ce", 16 * g + cc - 1)
                                for lc in range(4):
                                    ins = e.matmul(KB_[cc % 2][:],
                                                   v_wuk[:, lc, cc * 128:(cc + 1) * 128],
                                                   v_ckv[:, lc, :],
                                                   start=(lc == 0), stop=(lc == 3))
                                    if lc == 3:
                                        ins.then_inc(sems["s_c"], 1)
                            for t4 in range(4):
                                for cg in range(4):
                                    i = 4 * t4 + cg
                                    W("s_de", 16 * g + i - 1)
                                    for lc in range(4):
                                        ins = e.matmul(VB[i % 2][:],
                                                       v_ckv[:, lc, t4 * 128:(t4 + 1) * 128],
                                                       v_wuv[:, lc, cg * 512:(cg + 1) * 512],
                                                       start=(lc == 0), stop=(lc == 3))
                                        if lc == 3:
                                            ins.then_inc(sems["s_dmm"], 1)
                            if own(g):
                                W("s_be", 4 * (g + 1))
                                for lc in range(4):
                                    W("s_b2e", 4 * g + lc - 1)
                                    for dc in range(16):
                                        ins = e.matmul(CB[lc % 2][:],
                                                       v_wdq[:, dc, lc * 128:(lc + 1) * 128],
                                                       v_xnsb[:, dc, :],
                                                       start=(dc == 0), stop=(dc == 15))
                                        if dc == 15:
                                            ins.then_inc(sems["s_b2"], 1)
                                W("s_ce", 16 * (g + 1))
                                W("s_b2e", 4 * (g + 1))
                                for cc in range(16):
                                    W("s_c2e", 16 * g + cc - 1)
                                    for lc in range(4):
                                        ins = e.matmul(KB_[cc % 2][:],
                                                       v_wuq[:, lc, cc * 128:(cc + 1) * 128],
                                                       v_cq[:, lc, :],
                                                       start=(lc == 0), stop=(lc == 3))
                                        if lc == 3:
                                            ins.then_inc(sems["s_c2"], 1)
                                W("s_de", 16 * (g + 1))
                                for t4 in range(4):
                                    W("s_qrr", 4 * g + t4)
                                    for cc in range(2):
                                        for lc in range(4):
                                            ins = e.matmul(VB[cc][:],
                                                           v_cq[:, lc, t4 * 128:(t4 + 1) * 128],
                                                           v_wqr[:, lc, cc * 512:(cc + 1) * 512],
                                                           start=(lc == 0), stop=(lc == 3))
                                            if lc == 3:
                                                ins.then_inc(sems["s_qr"], 1)
                                    W("s_qrr", 4 * g + t4 + 1)
                                    W("s_qrte", 4 * g + t4 - 1)
                                    for c in range(8):
                                        ins = e.matmul(TB[c % 2][:, (c // 2) * 128:(c // 2 + 1) * 128],
                                                       qr_bf[:, c * 128:(c + 1) * 128],
                                                       ident[:], start=True, stop=True)
                                        if c == 7:
                                            ins.then_inc(sems["s_qrt"], 1)
                            W("s_krr", 4 * g + 4)
                            W("s_ce", 16 * (g + 1))
                            if own(g):
                                W("s_c2e", 16 * (g + 1))
                            W("s_krte", g)
                            for t4 in range(4):
                                ins = e.matmul(KB_[0][0:64, t4 * 128:(t4 + 1) * 128],
                                               v_krtm[:, t4, :], ident[:],
                                               start=True, stop=True)
                                if t4 == 3:
                                    ins.then_inc(sems["s_krt"], 1)
                        bar_end(e, W)

            if nstages < 2:
                return nc
            oall = es12.enter_context(nc.sbuf_tensor("oall", [128, H * 2048], BF16))
            wo_sb = es12.enter_context(nc.sbuf_tensor("wo_sb", [128, 16 * 2048], BF16))
            v_wo = wo_sb[:].rearrange("p (hc m) -> p hc m", hc=16)
            with ExitStack() as e2:
                def SB2(name, shape, dt=F32):
                    return e2.enter_context(nc.sbuf_tensor(name, shape, dt))

                def PS2(name, shape):
                    return e2.enter_context(nc.psum_tensor(name, shape, F32))

                khsb = [SB2(f"khsb{i}", [128, S], BF16) for i in range(2)]
                vhsb = [SB2(f"vhsb{i}", [128, S], BF16) for i in range(2)]
                qhsb = [SB2(f"qhsb{i}", [128, 2048], BF16) for i in range(2)]
                qrhsb = [SB2(f"qrhsb{i}", [128, 2048], BF16) for i in range(2)]
                kr_pad = SB2("kr_pad", [128, S], BF16)
                pt_big = SB2("pt_big", [128, 2048], BF16)
                rs_tmp = SB2("rs_tmp", [128, 512])
                rs_sb = SB2("rs_sb", [128, 512])

                sps_big = PS2("sps_big", [128, 2048])
                sumps = PS2("sumps", [128, 512])
                ops_ = PS2("ops", [128, 512])

                with nc.Block() as b2:
                    @b2.sync
                    def _(e):
                        W = _mkW(e, sems)
                        DMA = _mkDMA(e, W, sems)
                        bar_begin(W, 1)
                        for h in range(H):
                            p = h % 2
                            W("s_hdone", h - 1)
                            DMA("s_kl", khsb[p][:], kscr[h])
                            for vc in range(4):
                                DMA(f"s_vl{vc}", vhsb[p][:, vc * 1024:(vc + 1) * 1024]
                                    .rearrange("p (t d) -> p t d", t=8),
                                    vscr[vc * 8:(vc + 1) * 8, :, h * 128:(h + 1) * 128]
                                    .rearrange("t p d -> p t d"))
                            DMA("s_ql", qhsb[p][:], qscr[h])
                            DMA("s_qrl", qrhsb[p][0:64, :], qrscr[h // 2][(h % 2) * 64:(h % 2) * 64 + 64, :])
                        bar_end(e, W)

                    @b2.gpsimd
                    def _(e):
                        W = _mkW(e, sems)
                        DMA = _mkDMA(e, W, sems)
                        bar_begin(W, 1)
                        wsrc = wo.rearrange("(hc p) m -> p hc m", p=128)
                        DMA("s_wo", v_wo[:, 0:8, :], wsrc[:, 0:8, :])
                        DMA("s_wo2", v_wo[:, 8:16, :], wsrc[:, 8:16, :])
                        W("s_wo", 16)
                        W("s_wo2", 16)
                        bar_end(e, W)

                    @b2.scalar
                    def _(e):
                        W = _mkW(e, sems)
                        DMA = _mkDMA(e, W, sems)
                        bar_begin(W, 1)
                        for h in range(H):
                            for qc in range(4):
                                qi = 4 * h + qc
                                for pp in range(16):
                                    W("s_smm", 32 * qi + 2 * pp + 2)
                                    W("s_pv", 32 * qi + 2 * pp - 2)
                                    half = (pp % 2) * 1024
                                    e.activation(pt_big[:, half:half + 1024],
                                                 sps_big[:, half:half + 1024], AF.Exp,
                                                 scale=CS).then_inc(sems["s_exp"], 1)
                                W("s_pv", 32 * qi + 32)
                                W("s_oev", qi)
                                e.activation(rs_tmp[:], sumps[:], AF.Ln)
                                e.activation(rs_sb[:], rs_tmp[:], AF.Exp, scale=-1.0
                                             ).then_inc(sems["s_rs"], 1)
                        bar_end(e, W)

                    @b2.vector
                    def _(e):
                        W = _mkW(e, sems)
                        DMA = _mkDMA(e, W, sems)
                        bar_begin(W, 1)
                        e.memset(kr_pad[64:128, :], 0)
                        e.memset(qrhsb[0][64:128, :], 0)
                        e.memset(qrhsb[1][64:128, :], 0)
                        e.tensor_copy(kr_pad[0:64, :], kr_res[:]).then_inc(sems["s_pad"], 1)
                        for h in range(H):
                            for qc in range(4):
                                qi = 4 * h + qc
                                W("s_rs", qi + 1)
                                W("s_pv", 32 * qi + 32)
                                e.tensor_mul(oall[:, h * 2048 + qc * 512:h * 2048 + (qc + 1) * 512],
                                             ops_[:], rs_sb[:]).then_inc(sems["s_oev"], 1)
                        bar_end(e, W)

                    @b2.tensor
                    def _(e):
                        W = _mkW(e, sems)
                        DMA = _mkDMA(e, W, sems)
                        bar_begin(W, 1)
                        W("s_pad", 1)
                        for h in range(H):
                            p = h % 2
                            W("s_kl", 16 * (h + 1))
                            W("s_ql", 16 * (h + 1))
                            W("s_qrl", 16 * (h + 1))
                            for vc in range(4):
                                W(f"s_vl{vc}", 16 * (h + 1))
                            for qc in range(4):
                                qi = 4 * h + qc
                                W("s_oev", qi)

                                def pv(j):
                                    W("s_exp", 16 * qi + j // 2 + 1)
                                    ptj = pt_big[:, (j % 4) * 512:(j % 4 + 1) * 512]
                                    e.matmul(sumps[:], ones[:], ptj,
                                             start=(j == 0), stop=(j == 31))
                                    e.matmul(ops_[:], vhsb[p][:, j * 128:(j + 1) * 128],
                                             ptj,
                                             start=(j == 0), stop=(j == 31)
                                             ).then_inc(sems["s_pv"], 1)

                                for ks in range(32):
                                    W("s_exp", 16 * qi + ks // 2 - 1)
                                    sl = sps_big[:, (ks % 4) * 512:(ks % 4 + 1) * 512]
                                    e.matmul(sl, khsb[p][:, ks * 128:(ks + 1) * 128],
                                             qhsb[p][:, qc * 512:(qc + 1) * 512],
                                             start=True, stop=False)
                                    e.matmul(sl, kr_pad[:, ks * 128:(ks + 1) * 128],
                                             qrhsb[p][:, qc * 512:(qc + 1) * 512],
                                             start=False, stop=True).then_inc(sems["s_smm"], 1)
                                    if ks >= 3:
                                        pv(ks - 3)
                                for j in (29, 30, 31):
                                    pv(j)
                            e.nop().then_inc(sems["s_hdone"], 1)
                        bar_end(e, W)

            if nstages < 3:
                return nc
            with ExitStack() as e4:
                def SB4(name, shape, dt=F32):
                    return e4.enter_context(nc.sbuf_tensor(name, shape, dt))

                def PS4(name, shape):
                    return e4.enter_context(nc.psum_tensor(name, shape, F32))

                x4 = [SB4(f"x4_{i}", [128, D]) for i in range(2)]
                res_sb = [SB4(f"res_sb{i}", [128, D]) for i in range(2)]
                wop = [PS4("wop0", [128, 2048]), PS4("wop1", [128, 2048])]

                with nc.Block() as b4:
                    @b4.sync
                    def _(e):
                        W = _mkW(e, sems)
                        DMA = _mkDMA(e, W, sems)
                        bar_begin(W, 2)
                        for qt in range(NTO):
                            W("s_s4add", qt - 1)
                            DMA("s_xl", x4[qt % 2][:], xb[qt * 128:(qt + 1) * 128, :])
                        bar_end(e, W)

                    @b4.gpsimd
                    def _(e):
                        W = _mkW(e, sems)
                        DMA = _mkDMA(e, W, sems)
                        bar_begin(W, 2)
                        for qt in range(NTO):
                            W("s_s4add", qt + 1)
                            DMA("s_resw", resscr[qt], res_sb[qt % 2][:])
                        W("s_resw", 16 * NTO)
                        bar_end(e, W)

                    @b4.tensor
                    def _(e):
                        W = _mkW(e, sems)
                        DMA = _mkDMA(e, W, sems)
                        bar_begin(W, 2)
                        for qt in range(NTO):
                            W("s_s4add", qt - 1)
                            for hh in range(H):
                                for mg in range(4):
                                    ins = e.matmul(wop[qt % 2][:, mg * 512:(mg + 1) * 512],
                                                   oall[:, hh * 2048 + qt * 128:hh * 2048 + (qt + 1) * 128],
                                                   v_wo[:, hh, mg * 512:(mg + 1) * 512],
                                                   start=(hh == 0), stop=(hh == 15))
                                    if hh == 15 and mg == 3:
                                        ins.then_inc(sems["s_s4mm"], 1)
                        bar_end(e, W)

                    @b4.vector
                    def _(e):
                        W = _mkW(e, sems)
                        DMA = _mkDMA(e, W, sems)
                        bar_begin(W, 2)
                        for qt in range(NTO):
                            W("s_s4mm", qt + 1)
                            W("s_xl", 16 * (qt + 1))
                            W("s_resw", 16 * (qt - 1))
                            e.tensor_add(res_sb[qt % 2][:], wop[qt % 2][:], x4[qt % 2][:]
                                         ).then_inc(sems["s_s4add"], 1)
                        bar_end(e, W)

                    @b4.scalar
                    def _(e):
                        W = _mkW(e, sems)
                        DMA = _mkDMA(e, W, sems)
                        bar_begin(W, 2)
                        bar_end(e, W)


        if nstages < 4:
            return nc
        with ExitStack() as e3:
            def SB3(name, shape, dt=F32):
                return e3.enter_context(nc.sbuf_tensor(name, shape, dt))

            def PS3(name, shape):
                return e3.enter_context(nc.psum_tensor(name, shape, F32))

            acc = SB3("acc", [128, 8 * 2048])
            ffn_xn = SB3("ffn_xn", [128, 2 * 8192], BF16)
            gsb = [SB3(f"gsb{i}", [128, 16 * 512], BF16) for i in range(2)]
            usb = [SB3(f"usb{i}", [128, 16 * 512], BF16) for i in range(2)]
            dsb = [SB3(f"dsb{i}", [128, 4 * 2048], BF16) for i in range(2)]
            h_sb = SB3("h_sb", [128, 4 * 512])
            h2_sb = SB3("h2_sb", [128, 4 * 512], BF16)
            st_sb = SB3("st_sb", [128, 3 * 512], BF16)

            gps = [PS3("gps0", [128, 512]), PS3("gps1", [128, 512])]
            ups = [PS3("ups0", [128, 512]), PS3("ups1", [128, 512])]
            ffn_ps = PS3("ffn_ps", [128, 2048])

            def GB(m):
                return gps[m][:] if m < 2 else ffn_ps[:, (m - 2) * 512:(m - 1) * 512]

            def UB(m):
                return ups[m][:] if m < 2 else ffn_ps[:, (m) * 512:(m + 1) * 512]

            NSTEP = NBLK * NSC * 2

            def sdec(s):
                return s // (2 * NSC), (s // 2) % NSC, s % 2

            with nc.Block() as b3:
                @b3.sync
                def _(e):
                    W = _mkW(e, sems)
                    DMA = _mkDMA(e, W, sems)
                    bar_begin(W, 3)
                    for tt in range(8):
                        DMA("s_acc", acc[:, tt * 2048:(tt + 1) * 2048], resscr[tt])
                    for j in range(2):
                        DMA("s_xnl", ffn_xn[:, j * 8192:(j + 1) * 8192], xnscr[j])
                    W("s_scd", NSC)
                    for tt in range(8):
                        W("s_add", 15 * 8 + tt + 1)
                        DMA("s_out", out[tt * 128:(tt + 1) * 128, :],
                            acc[:, tt * 2048:(tt + 1) * 2048])
                        DMA("s_acc", acc[:, tt * 2048:(tt + 1) * 2048], resscr[8 + tt])
                    for j in range(2):
                        DMA("s_xnl", ffn_xn[:, j * 8192:(j + 1) * 8192], xnscr[2 + j])
                    for tt in range(8):
                        W("s_add", 128 + 15 * 8 + tt + 1)
                        DMA("s_out", out[(8 + tt) * 128:(8 + tt + 1) * 128, :],
                            acc[:, tt * 2048:(tt + 1) * 2048])
                    W("s_out", 16 * 16)
                    bar_end(e, W)

                @b3.gpsimd
                def _(e):
                    W = _mkW(e, sems)
                    DMA = _mkDMA(e, W, sems)
                    bar_begin(W, 3)
                    for blk in range(NBLK):
                        for sc in range(NSC):
                            scg = NSC * blk + sc
                            W("s_scd", scg - 1)
                            gdst = gsb[sc % 2][:].rearrange("p (dc i) -> p dc i", dc=16)
                            udst = usb[sc % 2][:].rearrange("p (dc i) -> p dc i", dc=16)
                            gsrc = wg[:, sc * 512:(sc + 1) * 512].rearrange("(dc p) i -> p dc i", p=128)
                            usrc = wu[:, sc * 512:(sc + 1) * 512].rearrange("(dc p) i -> p dc i", p=128)
                            for sp in range(2):
                                DMA("s_wg", gdst[:, sp * 8:(sp + 1) * 8, :], gsrc[:, sp * 8:(sp + 1) * 8, :])
                                DMA("s_wu", udst[:, sp * 8:(sp + 1) * 8, :], usrc[:, sp * 8:(sp + 1) * 8, :])
                            DMA("s_wdn", dsb[sc % 2][:].rearrange("p (ic m) -> p ic m", ic=4),
                                wd[sc * 512:(sc + 1) * 512, :].rearrange("(ic p) m -> p ic m", p=128))
                    bar_end(e, W)

                @b3.tensor
                def _(e):
                    W = _mkW(e, sems)
                    DMA = _mkDMA(e, W, sems)
                    bar_begin(W, 3)

                    def gu_m(s, m):
                        blk, sc, go2 = sdec(s)
                        scg = NSC * blk + sc
                        W("s_wg", 32 * (scg + 1))
                        W("s_wu", 32 * (scg + 1))
                        W("s_xnl", 32 * (blk + 1))
                        W("s_silu", 4 * (s - 1) + m + 1)
                        W("s_hmul", 4 * (s - 1) + m + 1)
                        if m >= 2:
                            W("s_add", 4 * (s - 1))
                        for dc in range(16):
                            e.matmul(GB(m),
                                     gsb[sc % 2][:, dc * 512 + m * 128:dc * 512 + m * 128 + 128],
                                     ffn_xn[:, go2 * 8192 + dc * 512:go2 * 8192 + (dc + 1) * 512],
                                     start=(dc == 0), stop=(dc == 15))
                        for dc in range(16):
                            ins = e.matmul(UB(m),
                                           usb[sc % 2][:, dc * 512 + m * 128:dc * 512 + m * 128 + 128],
                                           ffn_xn[:, go2 * 8192 + dc * 512:go2 * 8192 + (dc + 1) * 512],
                                           start=(dc == 0), stop=(dc == 15))
                            if dc == 15:
                                ins.then_inc(sems["s_gu"], 1)

                    def gu(s):
                        for m in range(4):
                            gu_m(s, m)

                    def down_tb(s, tb):
                        blk, sc, go2 = sdec(s)
                        scg = NSC * blk + sc
                        if tb == 0:
                            W("s_mask", s + 1)
                            W("s_wdn", 16 * (scg + 1))
                            if (s + 1) % (2 * NSC) != 0 and s + 1 < NSTEP:
                                W("s_silu", 4 * (s + 1) + 4)
                                W("s_hmul", 4 * (s + 1) + 4)
                        W("s_add", 4 * s + tb)
                        for m in range(4):
                            for mg in range(4):
                                ins = e.matmul(
                                    ffn_ps[:, mg * 512:(mg + 1) * 512],
                                    h2_sb[:, m * 512 + tb * 128:m * 512 + tb * 128 + 128],
                                    dsb[sc % 2][:, m * 2048 + mg * 512:m * 2048 + (mg + 1) * 512],
                                    start=(m == 0), stop=(m == 3))
                                if m == 3 and mg == 3:
                                    ins.then_inc(sems["s_down"], 1)
                        if tb == 3 and go2 == 1:
                            e.nop().then_inc(sems["s_scd"], 1)

                    NPB = 2 * NSC
                    for blk in range(NBLK):
                        b0 = blk * NPB
                        gu(b0)
                        gu(b0 + 1)
                        for s in range(b0, b0 + NPB):
                            nxt = s + 2 if s + 2 <= b0 + NPB - 1 else None
                            down_tb(s, 0)
                            if nxt is not None:
                                gu_m(nxt, 0)
                            down_tb(s, 1)
                            if nxt is not None:
                                gu_m(nxt, 1)
                            down_tb(s, 2)
                            down_tb(s, 3)
                            if nxt is not None:
                                gu_m(nxt, 2)
                                gu_m(nxt, 3)
                    bar_end(e, W)

                @b3.scalar
                def _(e):
                    W = _mkW(e, sems)
                    DMA = _mkDMA(e, W, sems)
                    bar_begin(W, 3)
                    for s in range(NSTEP):
                        for m in range(4):
                            if m == 0:
                                W("s_mask", s)
                            W("s_gu", 4 * s + m + 1)
                            e.activation(h_sb[:, m * 512:(m + 1) * 512], GB(m),
                                         AF.Silu).then_inc(sems["s_silu"], 1)
                    bar_end(e, W)

                @b3.vector
                def _(e):
                    W = _mkW(e, sems)
                    DMA = _mkDMA(e, W, sems)
                    bar_begin(W, 3)

                    def accadd(s, tb):
                        blk, sc, go2 = sdec(s)
                        tta = go2 * 4 + tb
                        W("s_acc", 16 * (blk * 8 + tta + 1))
                        W("s_down", 4 * s + tb + 1)
                        asl = acc[:, tta * 2048:(tta + 1) * 2048]
                        e.tensor_tensor(asl, ffn_ps[:], asl, op=ALU.add).then_inc(sems["s_add"], 1)

                    NPB3 = 2 * NSC
                    for s in range(NSTEP):
                        for m in range(4):
                            W("s_silu", 4 * s + m + 1)
                            hm = h_sb[:, m * 512:(m + 1) * 512]
                            e.tensor_mul(hm, hm, UB(m)).then_inc(sems["s_hmul"], 1)
                        if s > 0 and (s - 1) % NPB3 != NPB3 - 1:
                            for tb in range(4):
                                accadd(s - 1, tb)
                        W("s_down", 4 * s)
                        for m in range(4):
                            hm = h_sb[:, m * 512:(m + 1) * 512]
                            e.tensor_mul(h2_sb[:, m * 512:(m + 1) * 512], hm, hm)
                        a = [h2_sb[:, j * 512:(j + 1) * 512] for j in range(4)]
                        A = st_sb[:, 0:512]; B = st_sb[:, 512:1024]; Dd = st_sb[:, 1024:1536]
                        e.tensor_max(A, a[0], a[1])
                        e.tensor_max(B, a[2], a[3])
                        e.tensor_tensor(A, A, B, op=ALU.min)
                        e.tensor_tensor(B, a[0], a[1], op=ALU.min)
                        e.tensor_tensor(Dd, a[2], a[3], op=ALU.min)
                        e.tensor_max(B, B, Dd)
                        e.tensor_max(A, A, B)
                        e.tensor_tensor(h2_sb[:].rearrange("p (a c) -> p a c", a=4),
                                        h2_sb[:].rearrange("p (a c) -> p a c", a=4),
                                        A.unsqueeze(1).broadcast_to([128, 4, 512]),
                                        op=ALU.is_ge)
                        e.tensor_mul(h2_sb[:], h_sb[:], h2_sb[:]).then_inc(sems["s_mask"], 1)
                        if s % NPB3 == NPB3 - 1:
                            for tb in range(4):
                                accadd(s, tb)
                    bar_end(e, W)

    return nc


_ROPE_CACHE = {}


def _rope_tables():
    if "t" not in _ROPE_CACHE:
        inv = 1.0 / (10000.0 ** (np.arange(0, PD, 2, dtype=np.float64) / PD))
        t = np.arange(S, dtype=np.float64)
        fr = t[:, None] * inv[None, :]
        emb = np.concatenate([fr, fr], axis=1)
        cos = np.cos(emb).astype(np.float32)
        sin = np.sin(emb).astype(np.float32)
        ssin = sin.copy()
        ssin[:, :32] *= -1.0
        _ROPE_CACHE["t"] = (cos, ssin)
    return _ROPE_CACHE["t"]


def kernel(L0_attn_hidden_states, L0_attn_W_DKV, L0_attn_W_DQ, L0_attn_W_UK,
           L0_attn_W_UV, L0_attn_W_UQ, L0_attn_W_KR, L0_attn_W_QR, L0_attn_W_O,
           L0_ffn_w_gate, L0_ffn_w_up, L0_ffn_w_down):
    x = np.ascontiguousarray(np.asarray(L0_attn_hidden_states, dtype=np.float32))
    cos, ssin = _rope_tables()

    wg = np.asarray(L0_ffn_w_gate, np.float32).reshape(D, 16, 128, 4) \
        .transpose(0, 1, 3, 2).reshape(D, II)
    wu = np.asarray(L0_ffn_w_up, np.float32).reshape(D, 16, 128, 4) \
        .transpose(0, 1, 3, 2).reshape(D, II)
    wd = np.asarray(L0_ffn_w_down, np.float32).reshape(16, 128, 4, D) \
        .transpose(0, 2, 1, 3).reshape(II, D)
    wg = np.ascontiguousarray(wg)
    wu = np.ascontiguousarray(wu)
    wd = np.ascontiguousarray(wd)

    common = dict(
        wdkv=np.ascontiguousarray(L0_attn_W_DKV, np.float32),
        wdq=np.ascontiguousarray(L0_attn_W_DQ, np.float32),
        wuk=np.ascontiguousarray(L0_attn_W_UK, np.float32),
        wuv=np.ascontiguousarray(L0_attn_W_UV, np.float32),
        wuq=np.ascontiguousarray(L0_attn_W_UQ, np.float32),
        wkr=np.ascontiguousarray(L0_attn_W_KR, np.float32),
        wqr=np.ascontiguousarray(L0_attn_W_QR, np.float32),
        wo=np.ascontiguousarray(L0_attn_W_O, np.float32),
        wg=wg, wu=wu, wd=wd,
        identD=np.eye(128, dtype=ml_dtypes.bfloat16),
        onesD=np.ones((128, 128), dtype=ml_dtypes.bfloat16),
    )

    nc = build_nc(debug=False)
    in_maps = []
    for c in range(8):
        b, half = c // 2, c % 2
        if half == 0:
            xb = x[b]
            cosc, ssc = cos, ssin
        else:
            xb = np.concatenate([x[b, 2048:], x[b, :2048]], axis=0)
            cosc = np.concatenate([cos[2048:], cos[:2048]], axis=0)
            ssc = np.concatenate([ssin[2048:], ssin[:2048]], axis=0)
        m = dict(common)
        m["xb"] = np.ascontiguousarray(xb)
        m["cosT"] = np.ascontiguousarray(cosc.astype(ml_dtypes.bfloat16))
        m["ssinT"] = np.ascontiguousarray(ssc.astype(ml_dtypes.bfloat16))
        in_maps.append(m)

    res = run_bass_kernel_spmd(nc, in_maps, core_ids=list(range(8)))
    out = np.empty((4, S, D), dtype=np.float32)
    for c in range(8):
        b, half = c // 2, c % 2
        out[b, half * 2048:(half + 1) * 2048] = res.results[c]["out"]
    return out


# version 19
# speedup vs baseline: 1.2971x; 1.2971x over previous
import numpy as np
import ml_dtypes
from contextlib import ExitStack

import concourse.bass as bass
import concourse.mybir as mybir
from concourse.bass_utils import run_bass_kernel_spmd

F32 = mybir.dt.float32
BF16 = mybir.dt.bfloat16
AF = mybir.ActivationFunctionType
ALU = mybir.AluOpType
AX = mybir.AxisListType

S, D = 4096, 2048
H, HD, LAT, PD, DK = 16, 128, 512, 64, 192
II = 8192
CS = float(1.0 / np.sqrt(DK))
EPS = 1e-5
NT, NTO = 32, 16
NG, NGO = 8, 4
NSC = 16
NBLK = 2

SEM_NAMES = [
    "c_l", "c_w", "s_x", "s_lns", "s_var", "s_rstd", "s_ln",
    "s_tp", "s_tpe", "s_kr", "s_krr", "s_krt", "s_krte",
    "s_b", "s_be", "s_c", "s_ce", "s_dmm", "s_de",
    "s_b2", "s_b2e", "s_c2", "s_c2e",
    "s_qr", "s_qrr", "s_qrt", "s_qrte",
    "s_kw", "s_vw", "s_qw", "s_qrw", "s_xnw",
    "bar",
    "s_kl", "s_vl0", "s_vl1", "s_vl2", "s_vl3", "s_ql", "s_qrl",
    "s_smm", "s_exp", "s_pv", "s_oev", "s_ost", "s_hdone",
    "s_ol", "s_ol2", "s_wo", "s_wo2", "s_s4mm", "s_xl", "s_s4add", "s_resw", "c_w2",
    "s_acc", "s_xnl", "s_wg", "s_wu", "s_wdn", "s_gu", "s_silu", "s_hmul", "s_mask",
    "s_T", "s_hfme", "s_down", "s_add", "s_scd", "s_out", "s_dvef", "s_rs", "s_pad",
]

ENGS = ["tensor", "scalar", "vector", "gpsimd", "sync"]


def _mkW(eng, sems):
    known = {}

    def W(name, val):
        if val <= 0:
            return
        if known.get(name, 0) >= val:
            return
        eng.wait_ge(sems[name], val)
        known[name] = val
    return W


def _mkFL(e, sems):
    cnt = [0]

    def FL(ins):
        cnt[0] += 1
        ins.then_inc(sems["s_dvef"], 1)
        e.wait_ge(sems["s_dvef"], cnt[0])
    return FL


def _mkDMA(e, W, sems):
    cnt = {}

    def DMA(semname, out, in_):
        c = cnt.get(semname, 0)
        if c > 0:
            W(semname, c)
        e.dma_start(out=out, in_=in_).then_inc(sems[semname], 16)
        cnt[semname] = c + 16
    return DMA


def build_nc(debug=False, nstages=4, ng=8, detect_races=True):
    NG = ng
    NGO = min(ng, 4)
    NT = 4 * NG
    NTO = 4 * NGO
    nc = bass.Bass("TRN2", target_bir_lowering=False,
                   detect_race_conditions=detect_races)

    def din(name, shape, dt=F32):
        return nc.dram_tensor(name, shape, dt, kind="ExternalInput").ap()

    xb = din("xb", [S, D])
    wdkv = din("wdkv", [D, LAT]); wdq = din("wdq", [D, LAT])
    wuk = din("wuk", [LAT, D]); wuv = din("wuv", [LAT, D]); wuq = din("wuq", [LAT, D])
    wkr = din("wkr", [D, PD]); wqr = din("wqr", [LAT, H * PD])
    wo = din("wo", [D, D])
    wg = din("wg", [D, II]); wu = din("wu", [D, II]); wd = din("wd", [II, D])
    cosT = din("cosT", [S, PD], BF16); ssinT = din("ssinT", [S, PD], BF16)
    identD = din("identD", [128, 128], BF16)
    onesD = din("onesD", [128, 128], BF16)
    out = nc.dram_tensor("out", [2048, D], F32, kind="ExternalOutput").ap()

    skind = "ExternalOutput" if debug else "Internal"
    krdbg = nc.dram_tensor("krdbg", [64, S], BF16, kind=skind).ap() if debug else None
    s2dbg = nc.dram_tensor("s2dbg", [128, 8 * H], F32, kind=skind).ap() if debug else None
    kscr = nc.dram_tensor("kscr", [H, 128, S], BF16, kind=skind).ap()
    vscr = nc.dram_tensor("vscr", [32, 128, D], BF16, kind=skind).ap()
    qscr = nc.dram_tensor("qscr", [H, 128, 2048], BF16, kind=skind).ap()
    qrscr = nc.dram_tensor("qrscr", [8, 128, 2048], BF16, kind=skind).ap()
    xnscr = nc.dram_tensor("xnscr", [4, 128, 8192], BF16, kind=skind).ap()
    oscr = nc.dram_tensor("oscr", [H, 128, 2048], BF16, kind=skind).ap()
    resscr = nc.dram_tensor("resscr", [16, 128, D], F32, kind=skind).ap()

    es = ExitStack()
    sems = {}
    with es:
        for n in SEM_NAMES:
            sems[n] = es.enter_context(nc.semaphore(n))

        def SB(name, shape, dt=F32):
            return es.enter_context(nc.sbuf_tensor(name, shape, dt))

        def PS(name, shape):
            return es.enter_context(nc.psum_tensor(name, shape, F32))

        ident = SB("ident", [128, 128], BF16)

        barv = [0]

        def bar_end(eng, W):
            eng.nop().then_inc(sems["bar"], 1)

        def bar_begin(W, k):
            W("bar", 5 * k)

        with ExitStack() as es12:
            def SB12(name, shape, dt=F32):
                return es12.enter_context(nc.sbuf_tensor(name, shape, dt))

            kr_res = SB12("kr_res", [64, S], BF16)
            ones = SB12("ones", [128, 128], BF16)
            eps_sb = SB12("eps_sb", [128, 1])

            with ExitStack() as e1:
                def SB1(name, shape, dt=F32):
                    return e1.enter_context(nc.sbuf_tensor(name, shape, dt))

                def PS1(name, shape):
                    return e1.enter_context(nc.psum_tensor(name, shape, F32))

                wdkv_sb = SB1("wdkv_sb", [128, 16 * 512], BF16)
                wdq_sb = SB1("wdq_sb", [128, 16 * 512], BF16)
                wuk_sb = SB1("wuk_sb", [128, 4 * 2048], BF16)
                wuv_sb = SB1("wuv_sb", [128, 4 * 2048], BF16)
                wuq_sb = SB1("wuq_sb", [128, 4 * 2048], BF16)
                wkr_sb = SB1("wkr_sb", [128, 16 * 64], BF16)
                wqr_sb = SB1("wqr_sb", [128, 4 * 1024], BF16)
                cos_sb = SB1("cos_sb", [128, NT * 64], BF16)
                ssin_sb = SB1("ssin_sb", [128, NT * 64], BF16)
                xsb = [SB1(f"xsb{i}", [128, D]) for i in range(3)]
                scrap_bf = SB1("scrap_bf", [128, D], mybir.dt.float8e4)
                scrap2_bf = scrap_bf
                stats = SB1("stats", [128, 6])
                mean_sb = SB1("mean_sb", [128, 3])
                rstd_sb = SB1("rstd_sb", [128, 3])
                var_sb = SB1("var_sb", [128, 3])
                rln_sb = SB1("rln_sb", [128, 3])
                tmp_sb = SB1("tmp_sb", [128, 3])
                xn_bf = [SB1(f"xn_bf{i}", [128, D], BF16) for i in range(3)]
                xnsb = SB1("xnsb", [128, 16 * 512], BF16)
                ckv_sb = SB1("ckv_sb", [128, 4 * 512], BF16)
                cq_sb = SB1("cq_sb", [128, 4 * 512], BF16)
                kq_stage = SB1("kq_stage", [128, 16 * 512], BF16)
                vsb = [SB1(f"vsb{i}", [128, D], BF16) for i in range(2)]
                qrsb = SB1("qrsb", [128, 8 * 512], BF16)
                kr_tm = SB1("kr_tm", [128, 4 * 64], BF16)
                qr_bf = SB1("qr_bf", [128, 1024], BF16)
                rop1 = SB1("rop1", [128, 512])
                rop2 = SB1("rop2", [128, 512])

                TB = [PS1("TB0", [128, 512]), PS1("TB1", [128, 512])]
                CB = [PS1("CB0", [128, 512]), PS1("CB1", [128, 512])]
                KB_ = [PS1("KB0", [128, 512]), PS1("KB1", [128, 512])]
                VB = [PS1("VB0", [128, 512]), PS1("VB1", [128, 512])]

                v_wdkv = wdkv_sb[:].rearrange("p (dc l) -> p dc l", dc=16)
                v_wdq = wdq_sb[:].rearrange("p (dc l) -> p dc l", dc=16)
                v_wuk = wuk_sb[:].rearrange("p (lc c) -> p lc c", lc=4)
                v_wuv = wuv_sb[:].rearrange("p (lc c) -> p lc c", lc=4)
                v_wuq = wuq_sb[:].rearrange("p (lc c) -> p lc c", lc=4)
                v_wkr = wkr_sb[:].rearrange("p (dc c) -> p dc c", dc=16)
                v_wqr = wqr_sb[:].rearrange("p (lc c) -> p lc c", lc=4)
                v_xnsb = xnsb[:].rearrange("p (dc j) -> p dc j", dc=16)
                v_ckv = ckv_sb[:].rearrange("p (lc j) -> p lc j", lc=4)
                v_cq = cq_sb[:].rearrange("p (lc j) -> p lc j", lc=4)
                v_kq = kq_stage[:].rearrange("p (cc j) -> p cc j", cc=16)
                v_cos = cos_sb[:].rearrange("p (t d) -> p t d", t=NT)
                v_ssin = ssin_sb[:].rearrange("p (t d) -> p t d", t=NT)
                v_krtm = kr_tm[:].rearrange("p (t d) -> p t d", t=4)
                v_qrsb = qrsb[:].rearrange("p (c j) -> p c j", c=8)

                def own(g):
                    return g < NGO

                CW_TOT = 16 * (2 + 2 + 1 + 1 + 1 + 2 + 1)

                with nc.Block() as b1:
                    @b1.sync
                    def _(e):
                        W = _mkW(e, sems)
                        DMA = _mkDMA(e, W, sems)
                        DMA("c_l", ident[:], identD)
                        DMA("c_l", ones[:], onesD)
                        for t in range(3):
                            DMA("s_x", xsb[t % 3][:], xb[t * 128:(t + 1) * 128, :])
                        DMA("c_l", cos_sb[:].rearrange("p (t d) -> p t d", t=NT), cosT[0:NT * 128, :].rearrange("(t p) d -> p t d", p=128))
                        DMA("c_l", ssin_sb[:].rearrange("p (t d) -> p t d", t=NT), ssinT[0:NT * 128, :].rearrange("(t p) d -> p t d", p=128))
                        for t in range(3, NT):
                            W("s_ln", t - 2)
                            DMA("s_x", xsb[t % 3][:], xb[t * 128:(t + 1) * 128, :])
                        bar_end(e, W)

                    @b1.gpsimd
                    def _(e):
                        W = _mkW(e, sems)
                        DMA = _mkDMA(e, W, sems)
                        loads = [
                            (wdkv_sb, wdkv.rearrange("(dc p) l -> p dc l", p=128), 16, 2),
                            (wdq_sb, wdq.rearrange("(dc p) l -> p dc l", p=128), 16, 2),
                            (wuk_sb, wuk.rearrange("(lc p) c -> p lc c", p=128), 4, 1),
                            (wuv_sb, wuv.rearrange("(lc p) c -> p lc c", p=128), 4, 1),
                            (wuq_sb, wuq.rearrange("(lc p) c -> p lc c", p=128), 4, 1),
                            (wkr_sb, wkr.rearrange("(dc p) c -> p dc c", p=128), 16, 2),
                            (wqr_sb, wqr.rearrange("(lc p) c -> p lc c", p=128), 4, 1),
                        ]
                        ncw = [0]
                        for sb_t, src, nch, nsplit in loads:
                            dst = sb_t[:].rearrange("p (a c) -> p a c", a=nch)
                            step = nch // nsplit
                            for sp in range(nsplit):
                                DMA("c_w" if ncw[0] % 2 == 0 else "c_w2",
                                    dst[:, sp * step:(sp + 1) * step, :],
                                    src[:, sp * step:(sp + 1) * step, :])
                                ncw[0] += 1
                        for g in range(NG):
                            W("s_ce", 16 * (g + 1))
                            for half in range(2):
                                DMA("s_kw", kscr[:, :, g * 512:(g + 1) * 512]
                                        .rearrange("h p j -> p h j")[:, half * 8:(half + 1) * 8, :], v_kq[:, half * 8:(half + 1) * 8, :])
                            for t4 in range(4):
                                t = 4 * g + t4
                                W("s_de", 16 * g + 4 * (t4 + 1))
                                DMA("s_vw", vscr[t], vsb[t4 % 2][:])
                            if own(g):
                                W("s_c2e", 16 * (g + 1))
                                for half in range(2):
                                    DMA("s_qw", qscr[:, :, g * 512:(g + 1) * 512]
                                            .rearrange("h p j -> p h j")[:, half * 8:(half + 1) * 8, :], v_kq[:, half * 8:(half + 1) * 8, :])
                                W("s_qrte", 4 * (g + 1))
                                DMA("s_qrw", qrscr[:, :, g * 512:(g + 1) * 512].rearrange("c p j -> p c j"), v_qrsb)
                                W("s_tpe", 16 * (g + 1))
                                DMA("s_xnw", xnscr[g], xnsb[:])
                        if debug:
                            W("s_krte", NG)
                            DMA("c_w", krdbg[:, 0:NG * 512], kr_res[:, 0:NG * 512])
                            W("c_w", 96)
                        W("s_kw", 16 * NG * 2)
                        W("s_vw", 16 * NT)
                        W("s_qw", 16 * NGO * 2)
                        W("s_qrw", 16 * NGO)
                        W("s_xnw", 16 * NGO)
                        bar_end(e, W)

                    @b1.scalar
                    def _(e):
                        W = _mkW(e, sems)
                        DMA = _mkDMA(e, W, sems)

                        def stats_op(t):
                            if t >= NT:
                                return
                            W("s_x", 16 * (t + 1))
                            W("s_var", t - 2)
                            e.activation(scrap_bf[:], xsb[t % 3][:], AF.Square,
                                         accum_out=stats[:, 2 * (t % 3):2 * (t % 3) + 1])
                            e.activation(scrap2_bf[:], xsb[t % 3][:], AF.Identity,
                                         accum_out=stats[:, 2 * (t % 3) + 1:2 * (t % 3) + 2]
                                         ).then_inc(sems["s_lns"], 1)

                        def rstd_op(t):
                            if t >= NT:
                                return
                            W("s_var", t + 1)
                            W("s_ln", t - 2)
                            e.activation(rln_sb[:, t % 3:t % 3 + 1], var_sb[:, t % 3:t % 3 + 1],
                                         AF.Ln, bias=eps_sb[:, 0:1])
                            e.activation(rstd_sb[:, t % 3:t % 3 + 1], rln_sb[:, t % 3:t % 3 + 1],
                                         AF.Exp, scale=-0.5).then_inc(sems["s_rstd"], 1)

                        stats_op(0)
                        for g in range(NG):
                            for t4 in range(4):
                                t = 4 * g + t4
                                rstd_op(t)
                                stats_op(t + 1)
                                for tg in range(4):
                                    idx = 4 * t + tg
                                    W("s_tp", idx + 1)
                                    if t4 == 0 and tg == 0 and g >= 1 and own(g - 1):
                                        W("s_xnw", 16 * g)
                                    e.copy(v_xnsb[:, tg * 4:(tg + 1) * 4, t4 * 128:(t4 + 1) * 128],
                                           TB[idx % 2][:].rearrange("p (c j) -> p c j", c=4)
                                           ).then_inc(sems["s_tpe"], 1)
                            if own(g):
                                for t4 in range(4):
                                    W("s_qrt", 4 * g + t4 + 1)
                                    if t4 == 0 and g >= 1:
                                        W("s_qrw", 16 * g)
                                    e.copy(v_qrsb[:, 0:8:2, t4 * 128:(t4 + 1) * 128],
                                           TB[0][:].rearrange("p (c j) -> p c j", c=4))
                                    e.copy(v_qrsb[:, 1:8:2, t4 * 128:(t4 + 1) * 128],
                                           TB[1][:].rearrange("p (c j) -> p c j", c=4)
                                           ).then_inc(sems["s_qrte"], 1)
                        bar_end(e, W)

                    @b1.vector
                    def _(e):
                        W = _mkW(e, sems)
                        DMA = _mkDMA(e, W, sems)
                        FL = _mkFL(e, sems)

                        def var_op(t):
                            if t >= NT:
                                return
                            W("s_lns", t + 1)
                            W("s_ln", t - 2)
                            sl = slice(t % 3, t % 3 + 1)
                            ssq = stats[:, 2 * (t % 3):2 * (t % 3) + 1]
                            ssum = stats[:, 2 * (t % 3) + 1:2 * (t % 3) + 2]
                            FL(e.tensor_scalar_mul(mean_sb[:, sl], ssum, 1.0 / D))
                            e.tensor_scalar_mul(var_sb[:, sl], ssq, 1.0 / D)
                            FL(e.tensor_mul(tmp_sb[:, sl], mean_sb[:, sl], mean_sb[:, sl]))
                            e.tensor_sub(var_sb[:, sl], var_sb[:, sl], tmp_sb[:, sl]
                                         ).then_inc(sems["s_var"], 1)

                        def norm_op(t):
                            if t >= NT:
                                return
                            W("s_rstd", t + 1)
                            W("s_tp", 4 * (t - 2))
                            sl = slice(t % 3, t % 3 + 1)
                            e.tensor_scalar(xn_bf[t % 3][:], xsb[t % 3][:],
                                            mean_sb[:, sl], rstd_sb[:, sl],
                                            ALU.subtract, ALU.mult).then_inc(sems["s_ln"], 1)

                        def rope_k(t):
                            g, t4 = t // 4, t % 4
                            W("c_l", 64)
                            W("s_kr", t + 1)
                            if t4 == 0:
                                W("s_krt", g)
                            src = CB[t % 2][:, 0:64]
                            e.tensor_mul(rop1[:, 0:64], src, v_cos[:, t, :])
                            e.tensor_mul(rop2[:, 0:32], CB[t % 2][:, 32:64], v_ssin[:, t, 0:32])
                            e.tensor_mul(rop2[:, 32:64], CB[t % 2][:, 0:32], v_ssin[:, t, 32:64])
                            e.tensor_add(v_krtm[:, t4, :], rop1[:, 0:64], rop2[:, 0:64]
                                         ).then_inc(sems["s_krr"], 1)

                        e.memset(eps_sb[:], EPS)
                        var_op(0)
                        var_op(1)
                        norm_op(0)
                        for g in range(NG):
                            for t4 in range(4):
                                t = 4 * g + t4
                                if t4 < 3:
                                    norm_op(t + 1)
                                    var_op(t + 2)
                                rope_k(t)
                            for lc in range(4):
                                W("s_b", 4 * g + lc + 1)
                                e.tensor_copy(v_ckv[:, lc, :], CB[lc % 2][:]).then_inc(sems["s_be"], 1)
                            for cc in range(16):
                                W("s_c", 16 * g + cc + 1)
                                if cc == 0:
                                    W("s_kw", 32 * g)
                                    if g >= 1 and own(g - 1):
                                        W("s_qw", 32 * g)
                                e.tensor_copy(v_kq[:, cc, :], KB_[cc % 2][:]).then_inc(sems["s_ce"], 1)
                            for t4 in range(4):
                                for cg in range(4):
                                    i = 4 * t4 + cg
                                    W("s_dmm", 16 * g + i + 1)
                                    if cg == 0:
                                        W("s_vw", 16 * (4 * g + t4 - 1))
                                    e.tensor_copy(vsb[t4 % 2][:, cg * 512:(cg + 1) * 512],
                                                  VB[i % 2][:]).then_inc(sems["s_de"], 1)
                            if own(g):
                                for lc in range(4):
                                    W("s_b2", 4 * g + lc + 1)
                                    e.tensor_copy(v_cq[:, lc, :], CB[lc % 2][:]).then_inc(sems["s_b2e"], 1)
                                for cc in range(16):
                                    W("s_c2", 16 * g + cc + 1)
                                    if cc == 0:
                                        W("s_kw", 32 * (g + 1))
                                    e.tensor_copy(v_kq[:, cc, :], KB_[cc % 2][:]).then_inc(sems["s_c2e"], 1)
                            if own(g):
                                for t4 in range(4):
                                    t = 4 * g + t4
                                    W("s_qr", 8 * g + 2 * t4 + 2)
                                    W("s_qrt", 4 * g + t4)
                                    for cc in range(2):
                                        vps = VB[cc][:].rearrange("p (h d) -> p h d", h=8)
                                        cosb = v_cos[:, t, :].unsqueeze(1).broadcast_to([128, 8, 64])
                                        ssb0 = v_ssin[:, t, 0:32].unsqueeze(1).broadcast_to([128, 8, 32])
                                        ssb1 = v_ssin[:, t, 32:64].unsqueeze(1).broadcast_to([128, 8, 32])
                                        r1 = rop1[:, 0:512].rearrange("p (h d) -> p h d", h=8)
                                        r2 = rop2[:, 0:512].rearrange("p (h d) -> p h d", h=8)
                                        e.tensor_mul(r1, vps, cosb)
                                        e.tensor_mul(r2[:, :, 0:32], vps[:, :, 32:64], ssb0)
                                        e.tensor_mul(r2[:, :, 32:64], vps[:, :, 0:32], ssb1)
                                        e.tensor_add(qr_bf[:, cc * 512:(cc + 1) * 512],
                                                     rop1[:, 0:512], rop2[:, 0:512])
                                    e.engine_nop().then_inc(sems["s_qrr"], 1)
                            W("s_krt", g + 1)
                            e.tensor_copy(kr_res[:, g * 512:(g + 1) * 512],
                                          KB_[0][0:64, :]).then_inc(sems["s_krte"], 1)
                            norm_op(4 * g + 4)
                            var_op(4 * g + 5)
                        bar_end(e, W)

                    @b1.tensor
                    def _(e):
                        W = _mkW(e, sems)
                        DMA = _mkDMA(e, W, sems)
                        for g in range(NG):
                            for t4 in range(4):
                                t = 4 * g + t4
                                W("s_ln", t + 1)
                                if g >= 1 and own(g - 1):
                                    W("s_qrte", 4 * g)
                                for tg in range(4):
                                    idx = 4 * t + tg
                                    W("s_tpe", idx - 1)
                                    W("c_l", 16)
                                    for j in range(4):
                                        ins = e.matmul(
                                            TB[idx % 2][:, j * 128:(j + 1) * 128],
                                            xn_bf[t % 3][:, (tg * 4 + j) * 128:(tg * 4 + j + 1) * 128],
                                            ident[:], start=True, stop=True)
                                        if j == 3:
                                            ins.then_inc(sems["s_tp"], 1)
                                W("s_tpe", 4 * (t + 1))
                                W("c_w", 80)
                                W("c_w2", 80)
                                W("s_krr", t - 1)
                                for dc in range(16):
                                    ins = e.matmul(CB[t % 2][:, 0:64],
                                                   v_xnsb[:, dc, t4 * 128:(t4 + 1) * 128],
                                                   v_wkr[:, dc, :],
                                                   start=(dc == 0), stop=(dc == 15))
                                    if dc == 15:
                                        ins.then_inc(sems["s_kr"], 1)
                            W("s_tpe", 16 * (g + 1))
                            W("s_krr", 4 * g + 4)
                            for lc in range(4):
                                W("s_be", 4 * g + lc - 1)
                                for dc in range(16):
                                    ins = e.matmul(CB[lc % 2][:],
                                                   v_wdkv[:, dc, lc * 128:(lc + 1) * 128],
                                                   v_xnsb[:, dc, :],
                                                   start=(dc == 0), stop=(dc == 15))
                                    if dc == 15:
                                        ins.then_inc(sems["s_b"], 1)
                            W("s_be", 4 * (g + 1))
                            W("s_krte", g)
                            for cc in range(16):
                                W("s_ce", 16 * g + cc - 1)
                                for lc in range(4):
                                    ins = e.matmul(KB_[cc % 2][:],
                                                   v_wuk[:, lc, cc * 128:(cc + 1) * 128],
                                                   v_ckv[:, lc, :],
                                                   start=(lc == 0), stop=(lc == 3))
                                    if lc == 3:
                                        ins.then_inc(sems["s_c"], 1)
                            for t4 in range(4):
                                for cg in range(4):
                                    i = 4 * t4 + cg
                                    W("s_de", 16 * g + i - 1)
                                    for lc in range(4):
                                        ins = e.matmul(VB[i % 2][:],
                                                       v_ckv[:, lc, t4 * 128:(t4 + 1) * 128],
                                                       v_wuv[:, lc, cg * 512:(cg + 1) * 512],
                                                       start=(lc == 0), stop=(lc == 3))
                                        if lc == 3:
                                            ins.then_inc(sems["s_dmm"], 1)
                            if own(g):
                                W("s_be", 4 * (g + 1))
                                for lc in range(4):
                                    W("s_b2e", 4 * g + lc - 1)
                                    for dc in range(16):
                                        ins = e.matmul(CB[lc % 2][:],
                                                       v_wdq[:, dc, lc * 128:(lc + 1) * 128],
                                                       v_xnsb[:, dc, :],
                                                       start=(dc == 0), stop=(dc == 15))
                                        if dc == 15:
                                            ins.then_inc(sems["s_b2"], 1)
                                W("s_ce", 16 * (g + 1))
                                W("s_b2e", 4 * (g + 1))
                                for cc in range(16):
                                    W("s_c2e", 16 * g + cc - 1)
                                    for lc in range(4):
                                        ins = e.matmul(KB_[cc % 2][:],
                                                       v_wuq[:, lc, cc * 128:(cc + 1) * 128],
                                                       v_cq[:, lc, :],
                                                       start=(lc == 0), stop=(lc == 3))
                                        if lc == 3:
                                            ins.then_inc(sems["s_c2"], 1)
                                W("s_de", 16 * (g + 1))
                                for t4 in range(4):
                                    W("s_qrr", 4 * g + t4)
                                    for cc in range(2):
                                        for lc in range(4):
                                            ins = e.matmul(VB[cc][:],
                                                           v_cq[:, lc, t4 * 128:(t4 + 1) * 128],
                                                           v_wqr[:, lc, cc * 512:(cc + 1) * 512],
                                                           start=(lc == 0), stop=(lc == 3))
                                            if lc == 3:
                                                ins.then_inc(sems["s_qr"], 1)
                                    W("s_qrr", 4 * g + t4 + 1)
                                    W("s_qrte", 4 * g + t4 - 1)
                                    for c in range(8):
                                        ins = e.matmul(TB[c % 2][:, (c // 2) * 128:(c // 2 + 1) * 128],
                                                       qr_bf[:, c * 128:(c + 1) * 128],
                                                       ident[:], start=True, stop=True)
                                        if c == 7:
                                            ins.then_inc(sems["s_qrt"], 1)
                            W("s_krr", 4 * g + 4)
                            W("s_ce", 16 * (g + 1))
                            if own(g):
                                W("s_c2e", 16 * (g + 1))
                            W("s_krte", g)
                            for t4 in range(4):
                                ins = e.matmul(KB_[0][0:64, t4 * 128:(t4 + 1) * 128],
                                               v_krtm[:, t4, :], ident[:],
                                               start=True, stop=True)
                                if t4 == 3:
                                    ins.then_inc(sems["s_krt"], 1)
                        bar_end(e, W)

            if nstages < 2:
                return nc
            oall = es12.enter_context(nc.sbuf_tensor("oall", [128, H * 2048], BF16))
            wo_sb = es12.enter_context(nc.sbuf_tensor("wo_sb", [128, 16 * 2048], BF16))
            v_wo = wo_sb[:].rearrange("p (hc m) -> p hc m", hc=16)
            with ExitStack() as e2:
                def SB2(name, shape, dt=F32):
                    return e2.enter_context(nc.sbuf_tensor(name, shape, dt))

                def PS2(name, shape):
                    return e2.enter_context(nc.psum_tensor(name, shape, F32))

                khsb = [SB2(f"khsb{i}", [128, S], BF16) for i in range(2)]
                vhsb = [SB2(f"vhsb{i}", [128, S], BF16) for i in range(2)]
                qhsb = [SB2(f"qhsb{i}", [128, 2048], BF16) for i in range(2)]
                qrhsb = [SB2(f"qrhsb{i}", [128, 2048], BF16) for i in range(2)]
                kr_pad = SB2("kr_pad", [128, S], BF16)
                pt_big = SB2("pt_big", [128, 2048], BF16)
                rs_tmp = SB2("rs_tmp", [128, 512])
                rs_sb = SB2("rs_sb", [128, 512])

                sps_big = PS2("sps_big", [128, 2048])
                sumps = PS2("sumps", [128, 512])
                ops_ = PS2("ops", [128, 512])

                with nc.Block() as b2:
                    @b2.sync
                    def _(e):
                        W = _mkW(e, sems)
                        DMA = _mkDMA(e, W, sems)
                        bar_begin(W, 1)
                        for h in range(H):
                            p = h % 2
                            W("s_hdone", h - 1)
                            DMA("s_kl", khsb[p][:], kscr[h])
                            for vc in range(4):
                                DMA(f"s_vl{vc}", vhsb[p][:, vc * 1024:(vc + 1) * 1024]
                                    .rearrange("p (t d) -> p t d", t=8),
                                    vscr[vc * 8:(vc + 1) * 8, :, h * 128:(h + 1) * 128]
                                    .rearrange("t p d -> p t d"))
                            DMA("s_ql", qhsb[p][:], qscr[h])
                            DMA("s_qrl", qrhsb[p][0:64, :], qrscr[h // 2][(h % 2) * 64:(h % 2) * 64 + 64, :])
                        bar_end(e, W)

                    @b2.gpsimd
                    def _(e):
                        W = _mkW(e, sems)
                        DMA = _mkDMA(e, W, sems)
                        bar_begin(W, 1)
                        wsrc = wo.rearrange("(hc p) m -> p hc m", p=128)
                        DMA("s_wo", v_wo[:, 0:8, :], wsrc[:, 0:8, :])
                        DMA("s_wo2", v_wo[:, 8:16, :], wsrc[:, 8:16, :])
                        W("s_wo", 16)
                        W("s_wo2", 16)
                        bar_end(e, W)

                    @b2.scalar
                    def _(e):
                        W = _mkW(e, sems)
                        DMA = _mkDMA(e, W, sems)
                        bar_begin(W, 1)
                        for h in range(H):
                            for qc in range(4):
                                qi = 4 * h + qc
                                for pp in range(16):
                                    W("s_smm", 32 * qi + 2 * pp + 2)
                                    W("s_pv", 32 * qi + 2 * pp - 2)
                                    half = (pp % 2) * 1024
                                    e.activation(pt_big[:, half:half + 1024],
                                                 sps_big[:, half:half + 1024], AF.Exp,
                                                 scale=CS).then_inc(sems["s_exp"], 1)
                                W("s_pv", 32 * qi + 32)
                                W("s_oev", qi)
                                e.activation(rs_tmp[:], sumps[:], AF.Ln)
                                e.activation(rs_sb[:], rs_tmp[:], AF.Exp, scale=-1.0
                                             ).then_inc(sems["s_rs"], 1)
                        bar_end(e, W)

                    @b2.vector
                    def _(e):
                        W = _mkW(e, sems)
                        DMA = _mkDMA(e, W, sems)
                        bar_begin(W, 1)
                        e.memset(kr_pad[64:128, :], 0)
                        e.memset(qrhsb[0][64:128, :], 0)
                        e.memset(qrhsb[1][64:128, :], 0)
                        e.tensor_copy(kr_pad[0:64, :], kr_res[:]).then_inc(sems["s_pad"], 1)
                        for h in range(H):
                            for qc in range(4):
                                qi = 4 * h + qc
                                W("s_rs", qi + 1)
                                W("s_pv", 32 * qi + 32)
                                e.tensor_mul(oall[:, h * 2048 + qc * 512:h * 2048 + (qc + 1) * 512],
                                             ops_[:], rs_sb[:]).then_inc(sems["s_oev"], 1)
                        bar_end(e, W)

                    @b2.tensor
                    def _(e):
                        W = _mkW(e, sems)
                        DMA = _mkDMA(e, W, sems)
                        bar_begin(W, 1)
                        W("s_pad", 1)
                        for h in range(H):
                            p = h % 2
                            W("s_kl", 16 * (h + 1))
                            W("s_ql", 16 * (h + 1))
                            W("s_qrl", 16 * (h + 1))
                            for vc in range(4):
                                W(f"s_vl{vc}", 16 * (h + 1))
                            for qc in range(4):
                                qi = 4 * h + qc
                                W("s_oev", qi)

                                def pv(j):
                                    W("s_exp", 16 * qi + j // 2 + 1)
                                    ptj = pt_big[:, (j % 4) * 512:(j % 4 + 1) * 512]
                                    e.matmul(sumps[:], ones[:], ptj,
                                             start=(j == 0), stop=(j == 31))
                                    e.matmul(ops_[:], vhsb[p][:, j * 128:(j + 1) * 128],
                                             ptj,
                                             start=(j == 0), stop=(j == 31)
                                             ).then_inc(sems["s_pv"], 1)

                                for ks in range(32):
                                    W("s_exp", 16 * qi + ks // 2 - 1)
                                    sl = sps_big[:, (ks % 4) * 512:(ks % 4 + 1) * 512]
                                    e.matmul(sl, khsb[p][:, ks * 128:(ks + 1) * 128],
                                             qhsb[p][:, qc * 512:(qc + 1) * 512],
                                             start=True, stop=False)
                                    e.matmul(sl, kr_pad[:, ks * 128:(ks + 1) * 128],
                                             qrhsb[p][:, qc * 512:(qc + 1) * 512],
                                             start=False, stop=True).then_inc(sems["s_smm"], 1)
                                    if ks >= 3:
                                        pv(ks - 3)
                                for j in (29, 30, 31):
                                    pv(j)
                            e.nop().then_inc(sems["s_hdone"], 1)
                        bar_end(e, W)

            if nstages < 3:
                return nc
            with ExitStack() as e4:
                def SB4(name, shape, dt=F32):
                    return e4.enter_context(nc.sbuf_tensor(name, shape, dt))

                def PS4(name, shape):
                    return e4.enter_context(nc.psum_tensor(name, shape, F32))

                x4 = [SB4(f"x4_{i}", [128, D]) for i in range(2)]
                res_sb = [SB4(f"res_sb{i}", [128, D]) for i in range(2)]
                wop = [PS4("wop0", [128, 2048]), PS4("wop1", [128, 2048])]

                with nc.Block() as b4:
                    @b4.sync
                    def _(e):
                        W = _mkW(e, sems)
                        DMA = _mkDMA(e, W, sems)
                        bar_begin(W, 2)
                        for qt in range(NTO):
                            W("s_s4add", qt - 1)
                            DMA("s_xl", x4[qt % 2][:], xb[qt * 128:(qt + 1) * 128, :])
                        bar_end(e, W)

                    @b4.gpsimd
                    def _(e):
                        W = _mkW(e, sems)
                        DMA = _mkDMA(e, W, sems)
                        bar_begin(W, 2)
                        for qt in range(NTO):
                            W("s_s4add", qt + 1)
                            DMA("s_resw", resscr[qt], res_sb[qt % 2][:])
                        W("s_resw", 16 * NTO)
                        bar_end(e, W)

                    @b4.tensor
                    def _(e):
                        W = _mkW(e, sems)
                        DMA = _mkDMA(e, W, sems)
                        bar_begin(W, 2)
                        for qt in range(NTO):
                            W("s_s4add", qt - 1)
                            for hh in range(H):
                                for mg in range(4):
                                    ins = e.matmul(wop[qt % 2][:, mg * 512:(mg + 1) * 512],
                                                   oall[:, hh * 2048 + qt * 128:hh * 2048 + (qt + 1) * 128],
                                                   v_wo[:, hh, mg * 512:(mg + 1) * 512],
                                                   start=(hh == 0), stop=(hh == 15))
                                    if hh == 15 and mg == 3:
                                        ins.then_inc(sems["s_s4mm"], 1)
                        bar_end(e, W)

                    @b4.vector
                    def _(e):
                        W = _mkW(e, sems)
                        DMA = _mkDMA(e, W, sems)
                        bar_begin(W, 2)
                        for qt in range(NTO):
                            W("s_s4mm", qt + 1)
                            W("s_xl", 16 * (qt + 1))
                            W("s_resw", 16 * (qt - 1))
                            e.tensor_add(res_sb[qt % 2][:], wop[qt % 2][:], x4[qt % 2][:]
                                         ).then_inc(sems["s_s4add"], 1)
                        bar_end(e, W)

                    @b4.scalar
                    def _(e):
                        W = _mkW(e, sems)
                        DMA = _mkDMA(e, W, sems)
                        bar_begin(W, 2)
                        bar_end(e, W)


        if nstages < 4:
            return nc
        with ExitStack() as e3:
            def SB3(name, shape, dt=F32):
                return e3.enter_context(nc.sbuf_tensor(name, shape, dt))

            def PS3(name, shape):
                return e3.enter_context(nc.psum_tensor(name, shape, F32))

            acc = SB3("acc", [128, 8 * 2048])
            ffn_xn = SB3("ffn_xn", [128, 2 * 8192], BF16)
            gsb = [SB3(f"gsb{i}", [128, 16 * 512], BF16) for i in range(2)]
            usb = [SB3(f"usb{i}", [128, 16 * 512], BF16) for i in range(2)]
            dsb = [SB3(f"dsb{i}", [128, 4 * 2048], BF16) for i in range(2)]
            sg_sb = SB3("sg_sb", [128, 512])
            h_sb = SB3("h_sb", [128, 512])
            h2_sb = SB3("h2_sb", [128, 512])
            st_sb = SB3("st_sb", [128, 512])
            ge_sb = SB3("ge_sb", [128, 512])
            hs_bf = [SB3(f"hs_bf{i}", [128, 512], BF16) for i in range(2)]
            hfm_sb = [SB3(f"hfm_sb{i}", [128, 512], BF16) for i in range(2)]

            gps = [PS3("gps0", [128, 512]), PS3("gps1", [128, 512])]
            ups = [PS3("ups0", [128, 512]), PS3("ups1", [128, 512])]
            ffn_ps = PS3("ffn_ps", [128, 2048])

            NSG = NBLK * NSC * 8

            def n2blk(n):
                return n // (NSC * 8), (n // 8) % NSC, n % 8

            def xn_slice(n, dc):
                blk, sc, tt = n2blk(n)
                go, t4 = tt // 4, tt % 4
                base = go * 8192 + dc * 512 + t4 * 128
                return ffn_xn[:, base:base + 128]

            with nc.Block() as b3:
                @b3.sync
                def _(e):
                    W = _mkW(e, sems)
                    DMA = _mkDMA(e, W, sems)
                    bar_begin(W, 3)
                    for tt in range(8):
                        DMA("s_acc", acc[:, tt * 2048:(tt + 1) * 2048], resscr[tt])
                    for j in range(2):
                        DMA("s_xnl", ffn_xn[:, j * 8192:(j + 1) * 8192], xnscr[j])
                    W("s_scd", NSC)
                    for tt in range(8):
                        W("s_add", 15 * 8 + tt + 1)
                        DMA("s_out", out[tt * 128:(tt + 1) * 128, :],
                            acc[:, tt * 2048:(tt + 1) * 2048])
                        DMA("s_acc", acc[:, tt * 2048:(tt + 1) * 2048], resscr[8 + tt])
                    for j in range(2):
                        DMA("s_xnl", ffn_xn[:, j * 8192:(j + 1) * 8192], xnscr[2 + j])
                    for tt in range(8):
                        W("s_add", 128 + 15 * 8 + tt + 1)
                        DMA("s_out", out[(8 + tt) * 128:(8 + tt + 1) * 128, :],
                            acc[:, tt * 2048:(tt + 1) * 2048])
                    W("s_out", 16 * 16)
                    bar_end(e, W)

                @b3.gpsimd
                def _(e):
                    W = _mkW(e, sems)
                    DMA = _mkDMA(e, W, sems)
                    bar_begin(W, 3)
                    for blk in range(NBLK):
                        for sc in range(NSC):
                            scg = NSC * blk + sc
                            W("s_scd", scg - 1)
                            gdst = gsb[sc % 2][:].rearrange("p (dc i) -> p dc i", dc=16)
                            udst = usb[sc % 2][:].rearrange("p (dc i) -> p dc i", dc=16)
                            gsrc = wg[:, sc * 512:(sc + 1) * 512].rearrange("(dc p) i -> p dc i", p=128)
                            usrc = wu[:, sc * 512:(sc + 1) * 512].rearrange("(dc p) i -> p dc i", p=128)
                            for sp in range(2):
                                DMA("s_wg", gdst[:, sp * 8:(sp + 1) * 8, :], gsrc[:, sp * 8:(sp + 1) * 8, :])
                                DMA("s_wu", udst[:, sp * 8:(sp + 1) * 8, :], usrc[:, sp * 8:(sp + 1) * 8, :])
                            DMA("s_wdn", dsb[sc % 2][:].rearrange("p (ic m) -> p ic m", ic=4),
                                wd[sc * 512:(sc + 1) * 512, :].rearrange("(ic p) m -> p ic m", p=128))
                    bar_end(e, W)

                @b3.tensor
                def _(e):
                    W = _mkW(e, sems)
                    DMA = _mkDMA(e, W, sems)
                    bar_begin(W, 3)

                    def gu(n):
                        if n >= NSG:
                            return
                        blk, sc, tt = n2blk(n)
                        scg = NSC * blk + sc
                        W("s_wg", 32 * (scg + 1))
                        W("s_wu", 32 * (scg + 1))
                        W("s_wdn", 16 * (scg + 1))
                        W("s_xnl", 32 * (blk + 1))
                        W("s_hfme", n - 1)
                        W("s_hmul", n - 1)
                        for dc in range(16):
                            e.matmul(gps[tt % 2][:], xn_slice(n, dc),
                                     gsb[sc % 2][:, dc * 512:(dc + 1) * 512],
                                     start=(dc == 0), stop=(dc == 15))
                            ins = e.matmul(ups[tt % 2][:], xn_slice(n, dc),
                                           usb[sc % 2][:, dc * 512:(dc + 1) * 512],
                                           start=(dc == 0), stop=(dc == 15))
                            if dc == 15:
                                ins.then_inc(sems["s_gu"], 1)

                    def Tdown(n):
                        if n < 0:
                            return
                        blk, sc, tt = n2blk(n)
                        W("s_mask", n + 1)
                        for c in range(4):
                            ins = e.matmul(gps[tt % 2][:, c * 128:(c + 1) * 128],
                                           hs_bf[n % 2][:, c * 128:(c + 1) * 128], ident[:],
                                           start=True, stop=True)
                            if c == 3:
                                ins.then_inc(sems["s_T"], 1)

                    def down(n):
                        blk, sc, tt = n2blk(n)
                        W("s_hfme", n + 1)
                        W("s_add", n)
                        for ic in range(4):
                            for mg in range(4):
                                ins = e.matmul(ffn_ps[:, mg * 512:(mg + 1) * 512],
                                               hfm_sb[n % 2][:, ic * 128:(ic + 1) * 128],
                                               dsb[sc % 2][:, ic * 2048 + mg * 512:ic * 2048 + (mg + 1) * 512],
                                               start=(ic == 0), stop=(ic == 3))
                                if ic == 3 and mg == 3:
                                    ins.then_inc(sems["s_down"], 1)
                        if tt == 7:
                            e.nop().then_inc(sems["s_scd"], 1)

                    NPB = NSC * 8
                    for blk in range(NBLK):
                        b0 = blk * NPB
                        gu(b0)
                        gu(b0 + 1)
                        Tdown(b0)
                        for n in range(b0, b0 + NPB - 1):
                            if n + 2 <= b0 + NPB - 1:
                                gu(n + 2)
                            Tdown(n + 1)
                            down(n)
                        down(b0 + NPB - 1)
                    bar_end(e, W)

                @b3.scalar
                def _(e):
                    W = _mkW(e, sems)
                    DMA = _mkDMA(e, W, sems)
                    bar_begin(W, 3)
                    for n in range(NSG):
                        blk, sc, tt = n2blk(n)
                        W("s_gu", n + 1)
                        e.activation(sg_sb[:], gps[tt % 2][:], AF.Silu).then_inc(sems["s_silu"], 1)
                        W("s_T", n + 1)
                        W("s_down", n - 1)
                        e.copy(hfm_sb[n % 2][:], gps[tt % 2][:]).then_inc(sems["s_hfme"], 1)
                    bar_end(e, W)

                @b3.vector
                def _(e):
                    W = _mkW(e, sems)
                    DMA = _mkDMA(e, W, sems)
                    NPB3 = NSC * 8
                    bar_begin(W, 3)
                    for n in range(NSG):
                        blk, sc, tt = n2blk(n)
                        W("s_silu", n + 1)
                        e.tensor_mul(h_sb[:], sg_sb[:], ups[tt % 2][:]).then_inc(sems["s_hmul"], 1)
                        e.tensor_mul(h2_sb[:], h_sb[:], h_sb[:])
                        a = [h2_sb[:, j * 128:(j + 1) * 128] for j in range(4)]
                        s0 = st_sb[:, 0:128]; s1 = st_sb[:, 128:256]
                        s2 = st_sb[:, 256:384]; s3 = st_sb[:, 384:512]
                        e.tensor_max(s0, a[0], a[1])
                        e.tensor_tensor(s1, a[0], a[1], op=ALU.min)
                        e.tensor_max(s2, a[2], a[3])
                        e.tensor_tensor(s3, a[2], a[3], op=ALU.min)
                        e.tensor_max(s1, s1, s3)
                        e.tensor_tensor(s0, s0, s2, op=ALU.min)
                        e.tensor_max(s0, s0, s1)
                        e.tensor_tensor(ge_sb[:].rearrange("p (a c) -> p a c", a=4),
                                        h2_sb[:].rearrange("p (a c) -> p a c", a=4),
                                        s0.unsqueeze(1).broadcast_to([128, 4, 128]),
                                        op=ALU.is_ge)
                        e.tensor_mul(hs_bf[n % 2][:], h_sb[:], ge_sb[:]).then_inc(sems["s_mask"], 1)
                        def accadd(na):
                            blka, _, tta = n2blk(na)
                            W("s_acc", 16 * (blka * 8 + tta + 1))
                            W("s_down", na + 1)
                            asl = acc[:, tta * 2048:(tta + 1) * 2048]
                            e.tensor_tensor(asl, ffn_ps[:], asl, op=ALU.add).then_inc(sems["s_add"], 1)
                        if n % NPB3 > 0:
                            accadd(n - 1)
                        if n % NPB3 == NPB3 - 1:
                            accadd(n)
                    bar_end(e, W)

    return nc


_ROPE_CACHE = {}


def _rope_tables():
    if "t" not in _ROPE_CACHE:
        inv = 1.0 / (10000.0 ** (np.arange(0, PD, 2, dtype=np.float64) / PD))
        t = np.arange(S, dtype=np.float64)
        fr = t[:, None] * inv[None, :]
        emb = np.concatenate([fr, fr], axis=1)
        cos = np.cos(emb).astype(np.float32)
        sin = np.sin(emb).astype(np.float32)
        ssin = sin.copy()
        ssin[:, :32] *= -1.0
        _ROPE_CACHE["t"] = (cos, ssin)
    return _ROPE_CACHE["t"]


def kernel(L0_attn_hidden_states, L0_attn_W_DKV, L0_attn_W_DQ, L0_attn_W_UK,
           L0_attn_W_UV, L0_attn_W_UQ, L0_attn_W_KR, L0_attn_W_QR, L0_attn_W_O,
           L0_ffn_w_gate, L0_ffn_w_up, L0_ffn_w_down):
    x = np.ascontiguousarray(np.asarray(L0_attn_hidden_states, dtype=np.float32))
    cos, ssin = _rope_tables()

    wg = np.asarray(L0_ffn_w_gate, np.float32).reshape(D, 16, 128, 4) \
        .transpose(0, 1, 3, 2).reshape(D, II)
    wu = np.asarray(L0_ffn_w_up, np.float32).reshape(D, 16, 128, 4) \
        .transpose(0, 1, 3, 2).reshape(D, II)
    wd = np.asarray(L0_ffn_w_down, np.float32).reshape(16, 128, 4, D) \
        .transpose(0, 2, 1, 3).reshape(II, D)
    wg = np.ascontiguousarray(wg)
    wu = np.ascontiguousarray(wu)
    wd = np.ascontiguousarray(wd)

    common = dict(
        wdkv=np.ascontiguousarray(L0_attn_W_DKV, np.float32),
        wdq=np.ascontiguousarray(L0_attn_W_DQ, np.float32),
        wuk=np.ascontiguousarray(L0_attn_W_UK, np.float32),
        wuv=np.ascontiguousarray(L0_attn_W_UV, np.float32),
        wuq=np.ascontiguousarray(L0_attn_W_UQ, np.float32),
        wkr=np.ascontiguousarray(L0_attn_W_KR, np.float32),
        wqr=np.ascontiguousarray(L0_attn_W_QR, np.float32),
        wo=np.ascontiguousarray(L0_attn_W_O, np.float32),
        wg=wg, wu=wu, wd=wd,
        identD=np.eye(128, dtype=ml_dtypes.bfloat16),
        onesD=np.ones((128, 128), dtype=ml_dtypes.bfloat16),
    )

    nc = build_nc(debug=False)
    in_maps = []
    for c in range(8):
        b, half = c // 2, c % 2
        if half == 0:
            xb = x[b]
            cosc, ssc = cos, ssin
        else:
            xb = np.concatenate([x[b, 2048:], x[b, :2048]], axis=0)
            cosc = np.concatenate([cos[2048:], cos[:2048]], axis=0)
            ssc = np.concatenate([ssin[2048:], ssin[:2048]], axis=0)
        m = dict(common)
        m["xb"] = np.ascontiguousarray(xb)
        m["cosT"] = np.ascontiguousarray(cosc.astype(ml_dtypes.bfloat16))
        m["ssinT"] = np.ascontiguousarray(ssc.astype(ml_dtypes.bfloat16))
        in_maps.append(m)

    res = run_bass_kernel_spmd(nc, in_maps, core_ids=list(range(8)))
    out = np.empty((4, S, D), dtype=np.float32)
    for c in range(8):
        b, half = c // 2, c % 2
        out[b, half * 2048:(half + 1) * 2048] = res.results[c]["out"]
    return out
